# version 02 (final)
# baseline (speedup 1.0000x reference)
import math
import numpy as np
import concourse.bass as bass
import concourse.mybir as mybir
from concourse.bass_utils import run_bass_kernel_spmd
from contextlib import ExitStack

F32 = mybir.dt.float32
BF16 = mybir.dt.bfloat16
AF = mybir.ActivationFunctionType
ALU = mybir.AluOpType
AX = mybir.AxisListType

T = 2048
DM = 2048
FF = 1024
NE = 16
NH = 16
NKV = 4
HD = 128
PD = 16
NT = T // 128
NDC = DM // 128
LN_EPS = 1e-5
RMS_EPS = float(np.finfo(np.float32).eps)
RSQ_HD = 1.0 / math.sqrt(HD)

RUN_PHASES = ("ffn0", "attn", "ffn1")
DEBUG_DUMP = None
FFN_UPTO = None
FFN_EXPERTS = None
ATTN_UPTO = None
ATTN_HALVES = (0, 1)
FLASH_QT = 0


class Rg:
    __slots__ = ("w", "rd")

    def __init__(self):
        self.w = None
        self.rd = {}


class B:
    STEP = {"pe": 1, "act": 1, "dve": 1, "gq": 16, "sq": 16}
    SELF_SKIP = {"pe": 10 ** 9, "act": 8, "dve": 8, "gq": 10 ** 9, "sq": 10 ** 9}

    def __init__(self, nc, st):
        self.nc = nc
        self.sem = {n: st.enter_context(nc.semaphore("s_" + n)) for n in self.STEP}
        self.cnt = {n: 0 for n in self.STEP}
        self.q = {n: [] for n in self.STEP}
        self.wm = {}

    def _coll(self, need, tok):
        if tok is None:
            return
        e, c = tok
        if need.get(e, -1) < c:
            need[e] = c

    def emit(self, en, fn, reads=(), writes=(), inc=True,
             defer_writer=(), finalize_writer=()):
        need = {}
        for r in reads:
            self._coll(need, r.w)
        for w in writes:
            self._coll(need, w.w)
            for e2, c2 in w.rd.items():
                self._coll(need, (e2, c2))
        for w in defer_writer:
            self._coll(need, w.w)
            for e2, c2 in w.rd.items():
                self._coll(need, (e2, c2))
        for e2, c2 in need.items():
            if e2 == en and self.cnt[en] - c2 >= self.SELF_SKIP[en] * self.STEP[en]:
                continue
            if self.wm.get((en, e2), 0) >= c2:
                continue
            self.q[en].append(("w", e2, c2))
            self.wm[(en, e2)] = c2
        self.q[en].append(("i", fn, inc))
        if inc:
            self.cnt[en] += self.STEP[en]
            tok = (en, self.cnt[en])
        else:
            tok = (en, self.cnt[en] + self.STEP[en])
        for r in reads:
            if r.rd.get(en, -1) < tok[1]:
                r.rd[en] = tok[1]
        for w in writes:
            w.w = tok
            w.rd = {}
        for w in defer_writer:
            w.w = None
            w.rd = {}
        for w in finalize_writer:
            w.w = tok
            w.rd = {}
        return tok

    def wait(self, en, toks):
        for (e2, c2) in toks:
            if self.wm.get((en, e2), 0) >= c2:
                continue
            self.q[en].append(("w", e2, c2))
            self.wm[(en, e2)] = c2

    def replay(self, en):
        items = self.q[en]
        sems = self.sem
        step = self.STEP[en]
        mysem = sems[en]

        def f(obj):
            for it in items:
                if it[0] == "w":
                    obj.wait_ge(sems[it[1]], it[2])
                else:
                    inst = it[1](obj)
                    if it[2]:
                        inst.then_inc(mysem, step)
        return f


def build_nc():
    nc = bass.Bass("TRN2", target_bir_lowering=False, debug=False)
    dt = nc.dram_tensor
    x_in = dt("x_in", [T, DM], F32, kind="ExternalInput").ap()
    g3 = dt("g3", [NE, 8, 128, 2048], F32, kind="ExternalInput").ap()
    u3 = dt("u3", [NE, 8, 128, 2048], F32, kind="ExternalInput").ap()
    d3 = dt("d3", [NE, 4, 128, 4096], F32, kind="ExternalInput").ap()
    sg3 = dt("sg3", [8, 128, 2048], F32, kind="ExternalInput").ap()
    su3 = dt("su3", [8, 128, 2048], F32, kind="ExternalInput").ap()
    sd3 = dt("sd3", [4, 128, 4096], F32, kind="ExternalInput").ap()
    router_r = dt("router_r", [128, 256], F32, kind="ExternalInput").ap()
    rbias = dt("rbias", [1, NE], F32, kind="ExternalInput").ap()
    qg_in = dt("qg_in", [1, NH], F32, kind="ExternalInput").ap()
    wqT = dt("wqT", [DM, DM], F32, kind="ExternalInput").ap()
    wkT = dt("wkT", [DM, NKV * HD], F32, kind="ExternalInput").ap()
    wvT = dt("wvT", [DM, NKV * HD], F32, kind="ExternalInput").ap()
    wpT = dt("wpT", [DM, DM], F32, kind="ExternalInput").ap()
    cosr = dt("cosr", [128, 128], F32, kind="ExternalInput").ap()
    sinr = dt("sinr", [128, 128], F32, kind="ExternalInput").ap()
    maskc = dt("maskc", [128, 128], F32, kind="ExternalInput").ap()
    identf = dt("identf", [128, 128], F32, kind="ExternalInput").ap()
    onescol = dt("onescol", [1, 128], F32, kind="ExternalInput").ap()
    out = dt("out", [T, DM], F32, kind="ExternalOutput").ap()
    wp_bf = dt("wp_bf", [DM, DM], BF16, kind="Internal").ap()
    dbg = None
    if DEBUG_DUMP == "h":
        dbg = dt("dbg", [128, NDC * T], F32, kind="ExternalOutput").ap()
    elif DEBUG_DUMP == "comb":
        dbg = dt("dbg", [128, NT * 16], F32, kind="ExternalOutput").ap()
    elif DEBUG_DUMP == "a":
        dbg = dt("dbg", [128, 8 * T], F32, kind="ExternalOutput").ap()
    elif DEBUG_DUMP == "ar":
        dbg = dt("dbg", [128, 8 * T], F32, kind="ExternalOutput").ap()
    elif DEBUG_DUMP == "hr":
        dbg = dt("dbg", [128, NDC * T], F32, kind="ExternalOutput").ap()
    elif DEBUG_DUMP == "flash0":
        dbg = dt("dbg", [128, 4128], F32, kind="ExternalOutput").ap()

    with ExitStack() as st:
        e = st.enter_context
        XR = e(nc.sbuf_tensor("XR", [128, NT * DM], BF16))
        HR = e(nc.sbuf_tensor("HR", [128, NDC * T], BF16))
        AR = e(nc.sbuf_tensor("AR", [128, 8 * T], BF16))
        WR = e(nc.sbuf_tensor("WR", [128, 16384], BF16))
        SCR = e(nc.sbuf_tensor("SCR", [128, 2144], F32))
        combR = e(nc.sbuf_tensor("combR", [128, NT * 16], F32))
        statR = e(nc.sbuf_tensor("statR", [128, 160], F32))
        csR = e(nc.sbuf_tensor("csR", [128, 256], F32))
        mskR = e(nc.sbuf_tensor("mskR", [128, 128], F32))
        idR = e(nc.sbuf_tensor("idR", [128, 128], BF16))
        qgbR = e(nc.sbuf_tensor("qgbR", [128, 32], F32))
        rowR = e(nc.sbuf_tensor("rowR", [1, 160], F32))
        rtR = e(nc.sbuf_tensor("rtR", [128, 256], BF16))
        ptR = e(nc.sbuf_tensor("ptR", [128, 256], BF16))
        ybR = e(nc.sbuf_tensor("ybR", [128, 128], BF16))
        colR = e(nc.sbuf_tensor("colR", [128, 28], F32))

        pb = [e(nc.psum_tensor(f"pb{i}", [128, 512], F32)) for i in range(6)]
        pt = [e(nc.psum_tensor(f"pt{i}", [128, 128], BF16)) for i in range(2)]

        b = B(nc, st)

        x_rg = [[Rg() for _ in range(4)] for _ in range(NT)]
        h_rg = [[Rg() for _ in range(4)] for _ in range(NDC)]
        a_rg = [[Rg() for _ in range(4)] for _ in range(8)]
        bank_rg = [Rg() for _ in range(8)]
        wr_pg = [Rg() for _ in range(8)]
        scr_rg = [Rg() for _ in range(4)]
        rope_rg = [Rg(), Rg()]
        comb_rg = [Rg() for _ in range(NT)]
        stat_rg = Rg()
        cs_rg, msk_rg, id_rg, qgb_rg, row_rg, rt_rg = (
            Rg(), Rg(), Rg(), Rg(), Rg(), Rg())
        pt_st_rg = [Rg(), Rg()]
        yb_rg = Rg()
        col_eps = Rg()
        col_sm = [Rg(), Rg()]
        col_rms = [Rg(), Rg()]
        wp_bf_rg = Rg()

        def xv(tt, dc=None):
            base = tt * DM
            if dc is None:
                return XR[:, base:base + DM]
            return XR[:, base + dc * 512: base + (dc + 1) * 512]

        def hv(d, lo, width):
            return HR[:, d * T + lo: d * T + lo + width]

        def wr(pg, lo=0, width=2048):
            return WR[:, pg * 2048 + lo: pg * 2048 + lo + width]

        scores_v = SCR[:, 0:2048]
        ktmp_v = [SCR[:, 0:512], SCR[:, 512:1024]]
        rjunk_v = SCR[:, 1024:1152]
        sg_v = [SCR[:, 0:512], SCR[:, 512:1024]]
        sco_v = lambda tt: SCR[:, 1024 + tt * 16: 1024 + (tt + 1) * 16]
        m8_v = lambda tt: SCR[:, 1536 + tt * 8: 1536 + (tt + 1) * 8]
        rope_v = [SCR[:, 2048:2080], SCR[:, 2080:2112]]
        htm_v = [wr(6), wr(7)]

        scols = statR[:, 0:16]
        qcols = statR[:, 16:32]
        m16 = statR[:, 32:48]
        msq16 = statR[:, 48:64]
        var16 = statR[:, 64:80]
        std16 = statR[:, 80:96]
        rstd16 = statR[:, 96:112]
        mr16 = statR[:, 112:128]
        nmr16 = statR[:, 128:144]
        ssum16 = statR[:, 144:160]
        epsln_c = colR[:, 0:1]
        epsrms_c = colR[:, 1:2]
        half_c = colR[:, 2:3]
        nm_c = [colR[:, 3:4], colR[:, 4:5]]
        l_c = [colR[:, 5:6], colR[:, 6:7]]
        rl_c = [colR[:, 7:8], colR[:, 8:9]]
        rms4 = [colR[:, 9:13], colR[:, 17:21]]
        rms4b = [colR[:, 13:17], colR[:, 21:25]]

        def dma(dst, src):
            return lambda g, d=dst, s=src: g.dma_start(d, s)

        b.emit("sq", dma(csR[:, 0:128], cosr), writes=[cs_rg])
        b.emit("sq", dma(csR[:, 128:256], sinr), reads=[cs_rg])
        b.emit("sq", dma(mskR[:], maskc), writes=[msk_rg])
        b.emit("sq", dma(rowR[0:1, 0:128], onescol), writes=[row_rg])
        b.emit("sq", dma(rowR[0:1, 128:128 + NH], qg_in), reads=[row_rg])
        b.emit("sq", dma(rowR[0:1, 128 + NH:128 + 2 * NH], rbias), reads=[row_rg])
        row_rg.w = ("sq", b.cnt["sq"])
        b.emit("gq", dma(idR[:], identf), writes=[id_rg])
        b.emit("gq", dma(rtR[:], router_r), writes=[rt_rg])
        for tt in range(NT):
            b.emit("gq", dma(xv(tt), x_in[tt * 128:(tt + 1) * 128, :]),
                   defer_writer=x_rg[tt], finalize_writer=x_rg[tt])
        b.emit("dve", lambda v: v.memset(epsln_c, LN_EPS), writes=[col_eps])
        b.emit("dve", lambda v: v.memset(epsrms_c, RMS_EPS), reads=[col_eps])
        b.emit("dve", lambda v: v.memset(half_c, 0.5), reads=[col_eps])
        col_eps.w = ("dve", b.cnt["dve"])
        b.emit("pe", lambda t: t.matmul(
            pb[0][:, 0:16], rowR[0:1, 0:128], rowR[0:1, 128:128 + NH],
            start=True, stop=True, skip_group_check=True),
            reads=[row_rg], writes=[bank_rg[0]])
        b.emit("act", lambda s: s.activation(
            qgbR[:, 0:16], pb[0][:, 0:16], AF.Copy, scale=RSQ_HD),
            reads=[bank_rg[0]], writes=[qgb_rg])
        b.emit("pe", lambda t: t.matmul(
            pb[0][:, 0:16], rowR[0:1, 0:128],
            rowR[0:1, 128 + NH:128 + 2 * NH],
            start=True, stop=True, skip_group_check=True),
            reads=[row_rg, qgb_rg], writes=[bank_rg[0]])
        b.emit("dve", lambda v: v.tensor_copy(qgbR[:, 16:32], pb[0][:, 0:16]),
               reads=[bank_rg[0]], finalize_writer=[qgb_rg])
        qgs_col = lambda h: qgbR[:, h:h + 1]
        biasb_v = qgbR[:, 16:32]

        def layer_norm(tts):
            n = len(tts)
            for i, tt in enumerate(tts):
                b.emit("act", lambda s, tt=tt, i=i: s.activation(
                    scores_v, xv(tt), AF.Square, accum_out=qcols[:, i:i + 1]),
                    reads=x_rg[tt],
                    writes=scr_rg if i == 0 else (),
                    defer_writer=[stat_rg] if i == 0 else ())
                b.emit("dve", lambda v, tt=tt, i=i: v.reduce_sum(
                    scols[:, i:i + 1], xv(tt), axis=AX.X),
                    reads=x_rg[tt])
            tok_a = ("act", b.cnt["act"])
            tok_d = ("dve", b.cnt["dve"])
            stat_rg.w = tok_d
            b.emit("dve", lambda v: v.tensor_scalar(
                m16[:, 0:n], scols[:, 0:n], 1.0 / DM, None, op0=ALU.mult),
                reads=[stat_rg], writes=[stat_rg])
            b.wait("act", [tok_d])
            b.emit("act", lambda s: s.activation(msq16[:, 0:n], m16[:, 0:n], AF.Square),
                   reads=[stat_rg], writes=[stat_rg])
            b.wait("dve", [tok_a])
            b.emit("dve", lambda v: v.tensor_scalar(
                var16[:, 0:n], qcols[:, 0:n], 1.0 / DM, None, op0=ALU.mult),
                reads=[stat_rg], writes=[stat_rg])
            b.emit("dve", lambda v: v.tensor_tensor(
                var16[:, 0:n], var16[:, 0:n], msq16[:, 0:n], op=ALU.subtract),
                reads=[stat_rg], writes=[stat_rg])
            b.emit("act", lambda s: s.activation(
                std16[:, 0:n], var16[:, 0:n], AF.Sqrt, bias=epsln_c),
                reads=[stat_rg, col_eps], writes=[stat_rg])
            b.emit("dve", lambda v: v.reciprocal(rstd16[:, 0:n], std16[:, 0:n]),
                   reads=[stat_rg], writes=[stat_rg])
            b.emit("dve", lambda v: v.tensor_tensor(
                mr16[:, 0:n], m16[:, 0:n], rstd16[:, 0:n], op=ALU.mult),
                reads=[stat_rg], writes=[stat_rg])
            b.emit("dve", lambda v: v.tensor_scalar(
                nmr16[:, 0:n], mr16[:, 0:n], -1.0, None, op0=ALU.mult),
                reads=[stat_rg], writes=[stat_rg])
            for i, tt in enumerate(tts):
                hp = i % 2
                b.emit("act", lambda s, tt=tt, i=i, hp=hp: s.activation(
                    htm_v[hp], xv(tt), AF.Identity,
                    bias=nmr16[:, i:i + 1], scale=rstd16[:, i:i + 1]),
                    reads=x_rg[tt] + [stat_rg], writes=[wr_pg[6 + hp]])
                for d in range(NDC):
                    par = d % 2
                    b.emit("pe", lambda t, d=d, par=par, hp=hp: t.transpose(
                        pt[par][:], htm_v[hp][:, d * 128:(d + 1) * 128], idR[:]),
                        reads=[wr_pg[6 + hp], id_rg], writes=[bank_rg[6 + par]])
                    dst = hv(d, i * 128, 128)
                    hreg = h_rg[d][(i * 128) // 512]
                    if d % 2 == 0:
                        b.emit("dve", lambda v, dst=dst, par=par: v.tensor_copy(
                            dst, pt[par][:]),
                            reads=[bank_rg[6 + par]], writes=[hreg])
                    else:
                        b.emit("act", lambda s, dst=dst, par=par: s.copy(
                            dst, pt[par][:]),
                            reads=[bank_rg[6 + par]], writes=[hreg])

        def dump_dbg():
            if DEBUG_DUMP == "h":
                src, regs = HR, [r for row in h_rg for r in row]
            elif DEBUG_DUMP == "comb":
                src, regs = combR, comb_rg
            elif DEBUG_DUMP == "a":
                src, regs = AR, [r for row in a_rg for r in row]
            else:
                return
            b.emit("gq", dma(dbg, src[:]), reads=regs)

        def ffn_phase(phase):
            layer_norm(list(range(NT)))
            if FFN_UPTO == "ln":
                dump_dbg()
                return
            for tt in range(NT):
                for d in range(NDC):
                    b.emit("pe", lambda t, d=d, tt=tt: t.matmul(
                        pb[4][:, 0:16], hv(d, tt * 128, 128),
                        rtR[:, d * 16:(d + 1) * 16],
                        start=(d == 0), stop=(d == NDC - 1),
                        skip_group_check=True),
                        reads=[h_rg[d][(tt * 128) // 512], rt_rg],
                        defer_writer=[bank_rg[4]] if d == 0 else (),
                        finalize_writer=[bank_rg[4]] if d == NDC - 1 else (),
                        inc=(d == NDC - 1))
                b.emit("act", lambda s, tt=tt: s.activation(
                    sco_v(tt), pb[4][:, 0:16], AF.Sigmoid),
                    reads=[bank_rg[4]], writes=[scr_rg[2]] if tt == 0 else (),
                    defer_writer=() if tt == 0 else ())
            tok_sig = ("act", b.cnt["act"])
            scr_rg[2].w = tok_sig
            for tt in range(NT):
                b.emit("dve", lambda v, tt=tt: v.tensor_tensor(
                    combR[:, tt * 16:(tt + 1) * 16], sco_v(tt), biasb_v, op=ALU.add),
                    reads=[scr_rg[2], qgb_rg], writes=[comb_rg[tt]])
            for tt in range(NT):
                b.emit("dve", lambda v, tt=tt: v.max(
                    out=m8_v(tt), in_=combR[:, tt * 16:(tt + 1) * 16]),
                    reads=[comb_rg[tt]], writes=[scr_rg[3]] if tt == 0 else ())
            tok_m8 = ("dve", b.cnt["dve"])
            scr_rg[3].w = tok_m8
            for tt in range(NT):
                b.emit("dve", lambda v, tt=tt: v.tensor_scalar(
                    combR[:, tt * 16:(tt + 1) * 16], combR[:, tt * 16:(tt + 1) * 16],
                    m8_v(tt)[:, 3:4], None, op0=ALU.is_ge),
                    reads=[scr_rg[3], comb_rg[tt]], writes=[comb_rg[tt]])
            for tt in range(NT):
                b.emit("dve", lambda v, tt=tt: v.tensor_tensor(
                    combR[:, tt * 16:(tt + 1) * 16], combR[:, tt * 16:(tt + 1) * 16],
                    sco_v(tt), op=ALU.mult),
                    reads=[scr_rg[2], comb_rg[tt]], writes=[comb_rg[tt]])
            for tt in range(NT):
                b.emit("dve", lambda v, tt=tt: v.reduce_sum(
                    ssum16[:, tt:tt + 1], combR[:, tt * 16:(tt + 1) * 16], axis=AX.X),
                    reads=[comb_rg[tt], stat_rg])
            stat_rg.w = ("dve", b.cnt["dve"])
            b.emit("dve", lambda v: v.reciprocal(ssum16[:], ssum16[:]),
                   reads=[stat_rg], writes=[stat_rg])
            for tt in range(NT):
                b.emit("dve", lambda v, tt=tt: v.tensor_scalar(
                    combR[:, tt * 16:(tt + 1) * 16], combR[:, tt * 16:(tt + 1) * 16],
                    ssum16[:, tt:tt + 1], 0.5, op0=ALU.mult, op1=ALU.mult),
                    reads=[stat_rg, comb_rg[tt]], writes=[comb_rg[tt]])
            if FFN_UPTO == "router":
                dump_dbg()
                return

            items = []
            idx = {}
            for ee in range(NE + 1):
                for fc in range(8):
                    src_g = g3[ee, fc] if ee < NE else sg3[fc]
                    src_u = u3[ee, fc] if ee < NE else su3[fc]
                    k = ee * 8 + fc
                    idx[("g", ee, fc)] = len(items)
                    items.append((wr(k % 2), src_g, [wr_pg[k % 2]]))
                    idx[("u", ee, fc)] = len(items)
                    items.append((wr(2 + k % 2), src_u, [wr_pg[2 + k % 2]]))
                for dc in range(4):
                    src_d = d3[ee, dc] if ee < NE else sd3[dc]
                    k = ee * 4 + dc
                    pga, pgb = 4 + 2 * (k % 2), 5 + 2 * (k % 2)
                    idx[("d", ee, dc)] = len(items)
                    items.append((WR[:, pga * 2048:(pgb + 1) * 2048], src_d,
                                  [wr_pg[pga], wr_pg[pgb]]))
                if phase == 0 and ee == 0:
                    idx["wp"] = len(items)
                    items.append((wp_bf, wpT, [wp_bf_rg]))
            emitted = [0]

            def ensure(i, ahead=3):
                while emitted[0] <= min(i + ahead, len(items) - 1):
                    dst, src, regs = items[emitted[0]]
                    b.emit("gq", dma(dst, src), defer_writer=regs,
                           finalize_writer=regs)
                    emitted[0] += 1

            sel_experts = list(FFN_EXPERTS) if FFN_EXPERTS is not None \
                else list(range(NE + 1))
            for ee in range(NE + 1):
                if ee not in sel_experts:
                    continue
                if ee < NE:
                    c_col = lambda tt, ee=ee: combR[:, tt * 16 + ee:tt * 16 + ee + 1]
                    c_reg = lambda tt: comb_rg[tt]
                else:
                    c_col = lambda tt: half_c
                    c_reg = lambda tt: col_eps
                for fc in range(8):
                    kg = ee * 8 + fc
                    ensure(idx[("g", ee, fc)], ahead=2)
                    ensure(idx[("u", ee, fc)], ahead=2)
                    gp, up_ = wr_pg[kg % 2], wr_pg[2 + kg % 2]
                    gsv, usv = wr(kg % 2), wr(2 + kg % 2)
                    for tc in range(4):
                        pg_ = (fc * 4 + tc) % 2
                        pu_ = 2 + (fc * 4 + tc) % 2
                        for c in range(NDC):
                            b.emit("pe", lambda t, c=c, tc=tc, gsv=gsv, pg_=pg_:
                                   t.matmul(pb[pg_][:],
                                            gsv[:, c * 128:(c + 1) * 128],
                                            hv(c, tc * 512, 512),
                                            start=(c == 0), stop=(c == NDC - 1),
                                            skip_group_check=True),
                                   reads=[gp, h_rg[c][tc]],
                                   defer_writer=[bank_rg[pg_]] if c == 0 else (),
                                   finalize_writer=[bank_rg[pg_]] if c == NDC - 1 else (),
                                   inc=(c == NDC - 1))
                        for c in range(NDC):
                            b.emit("pe", lambda t, c=c, tc=tc, usv=usv, pu_=pu_:
                                   t.matmul(pb[pu_][:],
                                            usv[:, c * 128:(c + 1) * 128],
                                            hv(c, tc * 512, 512),
                                            start=(c == 0), stop=(c == NDC - 1),
                                            skip_group_check=True),
                                   reads=[up_, h_rg[c][tc]],
                                   defer_writer=[bank_rg[pu_]] if c == 0 else (),
                                   finalize_writer=[bank_rg[pu_]] if c == NDC - 1 else (),
                                   inc=(c == NDC - 1))
                        sp = (fc * 4 + tc) % 2
                        b.emit("act", lambda s, pg_=pg_, sp=sp: s.activation(
                            sg_v[sp], pb[pg_][:], AF.Silu),
                            reads=[bank_rg[pg_]], writes=[scr_rg[sp]])
                        a_dst = AR[:, fc * T + tc * 512: fc * T + (tc + 1) * 512]
                        b.emit("dve", lambda v, a_dst=a_dst, sp=sp, pu_=pu_:
                               v.tensor_tensor(a_dst, sg_v[sp], pb[pu_][:],
                                               op=ALU.mult),
                               reads=[scr_rg[sp], bank_rg[pu_]],
                               writes=[a_rg[fc][tc]])
                for dc in range(4):
                    kd = ee * 4 + dc
                    ensure(idx[("d", ee, dc)], ahead=1)
                    pga, pgb = 4 + 2 * (kd % 2), 5 + 2 * (kd % 2)
                    dsv = WR[:, pga * 2048:(pgb + 1) * 2048]
                    for tt in range(NT):
                        pk = (dc * NT + tt) % 2
                        bkv = pb[4 + pk]
                        for fc in range(8):
                            b.emit("pe", lambda t, fc=fc, tt=tt, dsv=dsv, bkv=bkv:
                                   t.matmul(bkv[:],
                                            AR[:, fc * T + tt * 128:
                                               fc * T + (tt + 1) * 128],
                                            dsv[:, fc * 512:(fc + 1) * 512],
                                            start=(fc == 0), stop=(fc == 7),
                                            skip_group_check=True),
                                   reads=[wr_pg[pga], wr_pg[pgb], a_rg[fc][tt // 4]],
                                   defer_writer=[bank_rg[4 + pk]] if fc == 0 else (),
                                   finalize_writer=[bank_rg[4 + pk]] if fc == 7 else (),
                                   inc=(fc == 7))
                        b.emit("dve", lambda v, tt=tt, dc=dc, bkv=bkv, cc=c_col(tt):
                               v.scalar_tensor_tensor(
                                   xv(tt, dc), bkv[:], cc, xv(tt, dc),
                                   op0=ALU.mult, op1=ALU.add),
                               reads=[bank_rg[4 + pk], c_reg(tt), x_rg[tt][dc]],
                               writes=[x_rg[tt][dc]])
            if phase == 0 and FFN_EXPERTS is None:
                ensure(idx["wp"], ahead=0)
            if DEBUG_DUMP == "a":
                dump_dbg()

        def attn_phase():
            if "ffn0" not in RUN_PHASES:
                b.emit("gq", dma(wp_bf, wpT), defer_writer=[wp_bf_rg],
                       finalize_writer=[wp_bf_rg])
            NTH = NT // 2
            v_rg = [Rg() for _ in range(NT)]
            k_rg = [Rg() for _ in range(NKV)]
            q_rg = [Rg() for _ in range(NH)]
            kall = lambda kv_: AR[:, kv_ * T:(kv_ + 1) * T]
            vall = lambda sc: AR[:, NKV * T + sc * 512: NKV * T + (sc + 1) * 512]
            qfm = lambda h_: HR[:, h_ * 2048 + 1024: h_ * 2048 + 2048]
            pbuf = [wr(3), wr(4)]
            pbuf_rg = [wr_pg[3], wr_pg[4]]
            wpsl = [wr(0), wr(1), wr(2)]
            wpsl_rg = [wr_pg[0], wr_pg[1], wr_pg[2]]

            def rms_rope(par, nheads, tt, dst_fn, dst_rg_fn):
                src_v = ktmp_v[par]
                src_rg_ = scr_rg[par]
                rms_ = rms4[par]
                rmsb_ = rms4b[par]
                crms = col_rms[par]
                rpv = rope_v[par]
                for h in range(nheads):
                    b.emit("act", lambda s, h=h, src_v=src_v, rms_=rms_:
                           s.activation(
                               rjunk_v, src_v[:, h * 128:(h + 1) * 128],
                               AF.Square, accum_out=rms_[:, h:h + 1]),
                           reads=[src_rg_],
                           writes=[scr_rg[2]] if h == 0 else (),
                           defer_writer=[crms] if h == 0 else ())
                tok = ("act", b.cnt["act"])
                crms.w = tok
                b.emit("dve", lambda v, n=nheads, rms_=rms_: v.tensor_scalar(
                    rms_[:, 0:n], rms_[:, 0:n], 1.0 / HD, epsrms_c,
                    op0=ALU.mult, op1=ALU.add),
                    reads=[crms, col_eps], writes=[crms])
                b.emit("act", lambda s, n=nheads, rms_=rms_, rmsb_=rmsb_:
                       s.activation(rmsb_[:, 0:n], rms_[:, 0:n], AF.Sqrt),
                       reads=[crms], writes=[crms])
                b.emit("dve", lambda v, n=nheads, rms_=rms_, rmsb_=rmsb_:
                       v.reciprocal(rms_[:, 0:n], rmsb_[:, 0:n]),
                       reads=[crms], writes=[crms])
                for h in range(nheads):
                    sp = h % 2
                    stg = ptR[:, sp * 128:(sp + 1) * 128]
                    b.emit("act", lambda s, h=h, stg=stg, src_v=src_v, rms_=rms_:
                           s.activation(
                               stg, src_v[:, h * 128:(h + 1) * 128], AF.Identity,
                               scale=rms_[:, h:h + 1]),
                           reads=[src_rg_, crms], writes=[pt_st_rg[sp]])
                    cos_v = csR[:, (tt % NT) * 8:(tt % NT) * 8 + 8]
                    sin_v = csR[:, 128 + (tt % NT) * 8:128 + (tt % NT) * 8 + 8]
                    t1, t2 = rpv[:, 0:8], rpv[:, 8:16]
                    t3, t4 = rpv[:, 16:24], rpv[:, 24:32]
                    rrg = rope_rg[par]
                    b.emit("dve", lambda v, stg=stg, cv=cos_v, t1=t1: v.tensor_tensor(
                        t1, stg[:, 0:8], cv, op=ALU.mult),
                        reads=[pt_st_rg[sp], cs_rg], writes=[rrg])
                    b.emit("dve", lambda v, stg=stg, sv=sin_v, t2=t2: v.tensor_tensor(
                        t2, stg[:, 8:16], sv, op=ALU.mult),
                        reads=[pt_st_rg[sp], cs_rg, rrg])
                    b.emit("dve", lambda v, stg=stg, sv=sin_v, t3=t3: v.tensor_tensor(
                        t3, stg[:, 0:8], sv, op=ALU.mult),
                        reads=[pt_st_rg[sp], cs_rg, rrg])
                    b.emit("dve", lambda v, stg=stg, cv=cos_v, t4=t4: v.tensor_tensor(
                        t4, stg[:, 8:16], cv, op=ALU.mult),
                        reads=[pt_st_rg[sp], cs_rg], writes=[rrg])
                    b.emit("dve", lambda v, stg=stg, t1=t1, t2=t2: v.tensor_tensor(
                        stg[:, 0:8], t1, t2, op=ALU.add),
                        reads=[rrg], writes=[pt_st_rg[sp]])
                    b.emit("dve", lambda v, stg=stg, t3=t3, t4=t4: v.tensor_tensor(
                        stg[:, 8:16], t3, t4, op=ALU.subtract),
                        reads=[rrg], writes=[pt_st_rg[sp]])
                    b.emit("pe", lambda t, stg=stg, sp=sp: t.transpose(
                        pt[sp][:], stg, idR[:]),
                        reads=[pt_st_rg[sp], id_rg], writes=[bank_rg[6 + sp]])
                    dst = dst_fn(h)
                    dreg = dst_rg_fn(h)
                    b.emit("dve", lambda v, dst=dst, sp=sp: v.tensor_copy(
                        dst, pt[sp][:]),
                        reads=[bank_rg[6 + sp]], writes=[dreg])

            def dump_attn(v_rg, k_rg, q_rg):
                if DEBUG_DUMP == "ar":
                    b.emit("gq", dma(dbg, AR[:]), reads=v_rg + k_rg)
                elif DEBUG_DUMP == "hr":
                    b.emit("gq", dma(dbg, HR[:]), reads=q_rg)

            for half in ATTN_HALVES:
                tts = list(range(half * NTH, (half + 1) * NTH))
                layer_norm(tts)
                for d in range(NDC):
                    pgi = d // 4
                    b.emit("gq", dma(wr(pgi, (d % 4) * 512, 512),
                                     wkT[d * 128:(d + 1) * 128, :]),
                           defer_writer=[wr_pg[pgi]] if d % 4 == 0 else (),
                           finalize_writer=[wr_pg[pgi]] if d % 4 == 3 else ())
                for d in range(NDC):
                    pgi = 4 + d // 4
                    b.emit("gq", dma(wr(pgi, (d % 4) * 512, 512),
                                     wvT[d * 128:(d + 1) * 128, :]),
                           defer_writer=[wr_pg[pgi]] if d % 4 == 0 else (),
                           finalize_writer=[wr_pg[pgi]] if d % 4 == 3 else ())

                def proj(i, tt, wtile_fn, wreg_fn, pk0, evac):
                    pk = pk0 + i % 2
                    for d in range(NDC):
                        b.emit("pe", lambda t, d=d, i=i, pk=pk: t.matmul(
                            pb[pk][:], hv(d, i * 128, 128), wtile_fn(d),
                            start=(d == 0), stop=(d == NDC - 1),
                            skip_group_check=True),
                            reads=[wreg_fn(d), h_rg[d][(i * 128) // 512]],
                            defer_writer=[bank_rg[pk]] if d == 0 else (),
                            finalize_writer=[bank_rg[pk]] if d == NDC - 1 else (),
                            inc=(d == NDC - 1))
                    evac(pk)

                for i, tt in enumerate(tts):
                    def evac_k(pk, i=i, tt=tt):
                        par = i % 2
                        b.emit("act", lambda s, pk=pk, par=par: s.copy(
                            ktmp_v[par], pb[pk][:]),
                            reads=[bank_rg[pk]], writes=[scr_rg[par]])
                        rms_rope(par, NKV, tt,
                                 lambda kv_, half=half, i=i: kall(kv_)[
                                     :, half * 1024 + i * 128:
                                     half * 1024 + i * 128 + 128],
                                 lambda kv_: k_rg[kv_])
                    proj(i, tt, lambda d: wr(d // 4, (d % 4) * 512, 512),
                         lambda d: wr_pg[d // 4], 0, evac_k)
                for i, tt in enumerate(tts):
                    def evac_v(pk, tt=tt):
                        b.emit("act", lambda s, pk=pk, tt=tt: s.copy(
                            vall(tt), pb[pk][:]),
                            reads=[bank_rg[pk]], writes=[v_rg[tt]])
                    proj(i, tt, lambda d: wr(4 + d // 4, (d % 4) * 512, 512),
                         lambda d: wr_pg[4 + d // 4], 0, evac_v)
                for fq in range(4):
                    par = fq % 2
                    for d in range(NDC):
                        pgi = par * 4 + d // 4
                        b.emit("gq", dma(wr(pgi, (d % 4) * 512, 512),
                                         wqT[d * 128:(d + 1) * 128,
                                             fq * 512:(fq + 1) * 512]),
                               defer_writer=[wr_pg[pgi]] if d % 4 == 0 else (),
                               finalize_writer=[wr_pg[pgi]] if d % 4 == 3 else ())
                    for i, tt in enumerate(tts):
                        def evac_q(pk, i=i, tt=tt, fq=fq):
                            par = i % 2
                            b.emit("act", lambda s, pk=pk, par=par: s.copy(
                                ktmp_v[par], pb[pk][:]),
                                reads=[bank_rg[pk]], writes=[scr_rg[par]])
                            rms_rope(par, 4, tt,
                                     lambda j, fq=fq, i=i: qfm(fq * 4 + j)[
                                         :, i * 128:(i + 1) * 128],
                                     lambda j, fq=fq: q_rg[fq * 4 + j])
                        proj(i, tt, lambda d, par=par: wr(
                            par * 4 + d // 4, (d % 4) * 512, 512),
                            lambda d, par=par: wr_pg[par * 4 + d // 4], 2, evac_q)

                if ATTN_UPTO == "proj":
                    dump_attn(v_rg, k_rg, q_rg)
                    continue

                def load_wp(head, slot):
                    b.emit("sq", dma(wpsl[slot],
                                     wp_bf[head * 128:(head + 1) * 128, :]),
                           reads=[wp_bf_rg], defer_writer=[wpsl_rg[slot]],
                           finalize_writer=[wpsl_rg[slot]])

                for qt in range(NTH):
                    gqt = half * NTH + qt
                    Kend = (gqt + 1) * 128
                    nkc = (Kend + 511) // 512
                    nsc = Kend // 128

                    def stage_A(head, pp, Kend=Kend, nkc=nkc, qt=qt):
                        if head >= NH:
                            return []
                        ops = []
                        kv_ = head // 4
                        sm = head % 2
                        for kc in range(nkc):
                            N = min(512, Kend - kc * 512)
                            ops.append(dict(
                                en="pe",
                                fn=lambda t, head=head, kc=kc, N=N, kv_=kv_, qt=qt:
                                t.matmul(pb[0][:, 0:N],
                                         qfm(head)[:, qt * 128:(qt + 1) * 128],
                                         kall(kv_)[:, kc * 512:kc * 512 + N],
                                         start=True, stop=True,
                                         skip_group_check=True),
                                reads=[q_rg[head], k_rg[kv_]],
                                writes=[bank_rg[0]]))
                            ops.append(dict(
                                en="act",
                                fn=lambda s, kc=kc, N=N, head=head:
                                s.activation(scores_v[:, kc * 512:kc * 512 + N],
                                             pb[0][:, 0:N], AF.Identity,
                                             scale=qgs_col(head)),
                                reads=[bank_rg[0], qgb_rg],
                                writes=[scr_rg[kc]]))
                        ops.append(dict(
                            en="dve",
                            fn=lambda v, Kend=Kend: v.tensor_tensor(
                                scores_v[:, Kend - 128:Kend],
                                scores_v[:, Kend - 128:Kend], mskR[:],
                                op=ALU.add),
                            reads=[msk_rg, scr_rg[(Kend - 128) // 512]],
                            writes=[scr_rg[(Kend - 128) // 512]]))
                        ops.append(dict(
                            en="dve",
                            fn=lambda v, Kend=Kend, sm=sm: v.reduce_max(
                                nm_c[sm], scores_v[:, 0:Kend], axis=AX.X,
                                negate=True),
                            reads=[scr_rg[i] for i in range(nkc)],
                            writes=[col_sm[sm]]))
                        ops.append(dict(
                            en="act",
                            fn=lambda s, Kend=Kend, pp=pp, sm=sm: s.activation(
                                pbuf[pp][:, 0:Kend], scores_v[:, 0:Kend], AF.Exp,
                                bias=nm_c[sm], accum_out=l_c[sm]),
                            reads=[col_sm[sm]] + [scr_rg[i] for i in range(nkc)],
                            writes=[pbuf_rg[pp], col_sm[sm]]))
                        ops.append(dict(
                            en="dve",
                            fn=lambda v, sm=sm: v.reciprocal(rl_c[sm], l_c[sm]),
                            reads=[col_sm[sm]], writes=[col_sm[sm]]))
                        ops.append(dict(
                            en="dve",
                            fn=lambda v, Kend=Kend, pp=pp, sm=sm: v.tensor_scalar(
                                pbuf[pp][:, 0:Kend], pbuf[pp][:, 0:Kend],
                                rl_c[sm], None, op0=ALU.mult),
                            reads=[col_sm[sm], pbuf_rg[pp]],
                            writes=[pbuf_rg[pp]]))
                        return ops

                    def stage_B(head, pp, nsc=nsc):
                        tops = []
                        mops = []
                        kv_ = head // 4
                        for sc in range(nsc):
                            par = sc % 2
                            tops.append([dict(
                                en="pe",
                                fn=lambda t, sc=sc, par=par, pp=pp: t.transpose(
                                    pt[par][:],
                                    pbuf[pp][:, sc * 128:(sc + 1) * 128], idR[:]),
                                reads=[pbuf_rg[pp], id_rg],
                                writes=[bank_rg[6 + par]]),
                                dict(
                                en="dve",
                                fn=lambda v, par=par: v.tensor_copy(
                                    ptR[:, par * 128:(par + 1) * 128], pt[par][:]),
                                reads=[bank_rg[6 + par]],
                                writes=[pt_st_rg[par]])])
                            mops.append(dict(
                                en="pe",
                                fn=lambda t, sc=sc, par=par, kv_=kv_: t.matmul(
                                    pb[1][:, 0:128],
                                    vall(sc)[:, kv_ * 128:(kv_ + 1) * 128],
                                    ptR[:, par * 128:(par + 1) * 128],
                                    start=(sc == 0), stop=(sc == nsc - 1),
                                    skip_group_check=True),
                                reads=[v_rg[sc], pt_st_rg[par]],
                                defer_writer=[bank_rg[1]] if sc == 0 else (),
                                finalize_writer=[bank_rg[1]] if sc == nsc - 1
                                else ()))
                        ops = []
                        ops.extend(tops[0])
                        for sc in range(nsc):
                            if sc + 1 < nsc:
                                ops.extend(tops[sc + 1])
                            ops.append(mops[sc])
                        ops.append(dict(
                            en="act",
                            fn=lambda s: s.copy(ybR[:], pb[1][:, 0:128]),
                            reads=[bank_rg[1]], writes=[yb_rg]))
                        slot = head % 3
                        for f in range(4):
                            ops.append(dict(
                                en="pe",
                                fn=lambda t, f=f, head=head, slot=slot: t.matmul(
                                    pb[2 + f][:], ybR[:],
                                    wpsl[slot][:, f * 512:(f + 1) * 512],
                                    start=(head == 0), stop=(head == NH - 1),
                                    skip_group_check=True),
                                reads=[yb_rg, wpsl_rg[slot]],
                                defer_writer=[bank_rg[2 + f]] if head == 0 else (),
                                finalize_writer=[bank_rg[2 + f]]
                                if head == NH - 1 else ()))
                        return ops

                    def emit_op(op):
                        b.emit(op["en"], op["fn"], reads=op.get("reads", ()),
                               writes=op.get("writes", ()),
                               defer_writer=op.get("defer_writer", ()),
                               finalize_writer=op.get("finalize_writer", ()))

                    if qt == 0:
                        load_wp(0, 0)
                        load_wp(1, 1)
                    for op in stage_A(0, 0):
                        emit_op(op)
                    if DEBUG_DUMP == "flash0" and half == 0 and qt == FLASH_QT:
                        b.emit("gq", dma(dbg[:, 0:2048], scores_v),
                               reads=[scr_rg[i] for i in range(4)])
                        b.emit("gq", dma(dbg[:, 2048:4096], pbuf[0][:]),
                               reads=[pbuf_rg[0]])
                        b.emit("gq", lambda g: g.dma_start(
                            dbg[:, 4096:4097], nm_c[0],
                            allow_slow_non_contiguous=True), reads=[col_sm[0]])
                        b.emit("gq", lambda g: g.dma_start(
                            dbg[:, 4097:4098], l_c[0],
                            allow_slow_non_contiguous=True), reads=[col_sm[0]])
                        b.emit("gq", lambda g: g.dma_start(
                            dbg[:, 4098:4099], rl_c[0],
                            allow_slow_non_contiguous=True), reads=[col_sm[0]])
                        b.emit("gq", dma(dbg[:, 4099:4115], qgbR[:, 0:16]),
                               reads=[qgb_rg])
                    for head in range(NH):
                        nxt = stage_A(head + 1, (head + 1) % 2)
                        bops = stage_B(head, head % 2)
                        ai = 0
                        for j, op in enumerate(bops):
                            emit_op(op)
                            if j % 3 == 2 and ai < len(nxt):
                                emit_op(nxt[ai])
                                ai += 1
                        for op in nxt[ai:]:
                            emit_op(op)
                        if head + 2 < NH:
                            load_wp(head + 2, (head + 2) % 3)
                    if qt + 1 < NTH:
                        load_wp(0, 0)
                        load_wp(1, 1)
                    for f in range(4):
                        b.emit("dve", lambda v, f=f, gqt=gqt: v.tensor_tensor(
                            xv(gqt, f), xv(gqt, f), pb[2 + f][:], op=ALU.add),
                            reads=[bank_rg[2 + f], x_rg[gqt][f]],
                            writes=[x_rg[gqt][f]])

        for ph in RUN_PHASES:
            if ph == "ffn0":
                ffn_phase(0)
            elif ph == "attn":
                attn_phase()
            elif ph == "ffn1":
                ffn_phase(1)

        for tt in range(NT):
            b.emit("gq", dma(out[tt * 128:(tt + 1) * 128, :], xv(tt)),
                   reads=x_rg[tt])
        b.wait("gq", [("gq", b.cnt["gq"])])

        blk = e(nc.Block())
        blk.tensor(b.replay("pe"))
        blk.scalar(b.replay("act"))
        blk.vector(b.replay("dve"))
        blk.gpsimd(b.replay("gq"))
        blk.sync(b.replay("sq"))
    return nc


def _prep_inputs(inp):
    asc = np.ascontiguousarray
    f32 = lambda a: np.asarray(a, np.float32)

    def relay_gu3(wT3):
        E_ = wT3.shape[0]
        return asc(wT3.reshape(E_, 16, 128, 8, 128).transpose(0, 3, 2, 1, 4)
                   .reshape(E_, 8, 128, 2048))

    def relay_dn3(wT3):
        E_ = wT3.shape[0]
        return asc(wT3.reshape(E_, 8, 128, 4, 512).transpose(0, 3, 2, 1, 4)
                   .reshape(E_, 4, 128, 4096))

    gateT = f32(inp["L0_ffn_expert_gate_w"]).transpose(0, 2, 1)
    upT = f32(inp["L0_ffn_expert_up_w"]).transpose(0, 2, 1)
    downT = f32(inp["L0_ffn_expert_down_w"]).transpose(0, 2, 1)
    g3 = relay_gu3(gateT)
    u3 = relay_gu3(upT)
    d3 = relay_dn3(downT)
    sg3 = relay_gu3(f32(inp["L0_ffn_shared_gate_w"]).T[None])[0]
    su3 = relay_gu3(f32(inp["L0_ffn_shared_up_w"]).T[None])[0]
    sd3 = relay_dn3(f32(inp["L0_ffn_shared_down_w"]).T[None])[0]
    router_r = asc(f32(inp["L0_ffn_router_w"]).T.reshape(16, 128, 16)
                   .transpose(1, 0, 2).reshape(128, 256))

    t = np.arange(T, dtype=np.float32)
    inv = 1.0 / (10000.0 ** (np.arange(0, PD, 2, dtype=np.float32) / PD))
    f = t[:, None] * inv[None, :]
    cosr = asc(np.cos(f).astype(np.float32).reshape(16, 128, 8)
               .transpose(1, 0, 2).reshape(128, 128))
    sinr = asc(np.sin(f).astype(np.float32).reshape(16, 128, 8)
               .transpose(1, 0, 2).reshape(128, 128))
    rr, cc = np.meshgrid(np.arange(128), np.arange(128), indexing="ij")
    maskc = np.where(cc <= rr, 0.0, -1e30).astype(np.float32)
    common = dict(
        g3=g3, u3=u3, d3=d3, sg3=sg3, su3=su3, sd3=sd3, router_r=router_r,
        rbias=f32(inp["L0_ffn_router_bias"]).reshape(1, NE),
        qg_in=f32(inp["L0_attn_qg"]).reshape(1, NH),
        wqT=asc(f32(inp["L0_attn_wq"]).T),
        wkT=asc(f32(inp["L0_attn_wk"]).T),
        wvT=asc(f32(inp["L0_attn_wv"]).T),
        wpT=asc(f32(inp["L0_attn_wp"]).T),
        cosr=cosr, sinr=sinr, maskc=maskc,
        identf=np.eye(128, dtype=np.float32),
        onescol=np.ones((1, 128), np.float32),
    )
    x = f32(inp["L0_attn_x"])
    in_maps = []
    for c in range(8):
        m = dict(common)
        m["x_in"] = asc(x[c])
        in_maps.append(m)
    return in_maps


def kernel(**inputs):
    in_maps = _prep_inputs(inputs)
    nc = build_nc()
    res = run_bass_kernel_spmd(nc, in_maps, core_ids=list(range(8)))
    outs = [np.asarray(res.results[c]["out"]) for c in range(8)]
    return np.stack(outs, axis=0).astype(np.float32)


# version 08
# speedup vs baseline: 1.0262x; 1.0262x over previous
import math
import numpy as np
import concourse.bass as bass
import concourse.mybir as mybir
from concourse.bass_utils import run_bass_kernel_spmd
from contextlib import ExitStack

F32 = mybir.dt.float32
BF16 = mybir.dt.bfloat16
AF = mybir.ActivationFunctionType
ALU = mybir.AluOpType
AX = mybir.AxisListType

T = 2048
DM = 2048
FF = 1024
NE = 16
NH = 16
NKV = 4
HD = 128
PD = 16
NT = T // 128
NDC = DM // 128
LN_EPS = 1e-5
RMS_EPS = float(np.finfo(np.float32).eps)
RSQ_HD = 1.0 / math.sqrt(HD)

RUN_PHASES = ("ffn0", "attn", "ffn1")
DEBUG_DUMP = None
FFN_UPTO = None
FFN_EXPERTS = None
ATTN_UPTO = None
ATTN_HALVES = (0, 1)
FLASH_QT = 0


class Rg:
    __slots__ = ("w", "rd")

    def __init__(self):
        self.w = None
        self.rd = {}


class B:
    STEP = {"pe": 1, "act": 1, "dve": 1, "gq": 16, "sq": 16}
    SELF_SKIP = {"pe": 10 ** 9, "act": 8, "dve": 8, "gq": 10 ** 9, "sq": 10 ** 9}

    def __init__(self, nc, st):
        self.nc = nc
        self.sem = {n: st.enter_context(nc.semaphore("s_" + n)) for n in self.STEP}
        self.cnt = {n: 0 for n in self.STEP}
        self.q = {n: [] for n in self.STEP}
        self.wm = {}

    def _coll(self, need, tok):
        if tok is None:
            return
        e, c = tok
        if need.get(e, -1) < c:
            need[e] = c

    def emit(self, en, fn, reads=(), writes=(), inc=True,
             defer_writer=(), finalize_writer=()):
        need = {}
        for r in reads:
            self._coll(need, r.w)
        for w in writes:
            self._coll(need, w.w)
            for e2, c2 in w.rd.items():
                self._coll(need, (e2, c2))
        for w in defer_writer:
            self._coll(need, w.w)
            for e2, c2 in w.rd.items():
                self._coll(need, (e2, c2))
        for e2, c2 in need.items():
            if e2 == en and self.cnt[en] - c2 >= self.SELF_SKIP[en] * self.STEP[en]:
                continue
            if self.wm.get((en, e2), 0) >= c2:
                continue
            self.q[en].append(("w", e2, c2))
            self.wm[(en, e2)] = c2
        self.q[en].append(("i", fn, inc))
        if inc:
            self.cnt[en] += self.STEP[en]
            tok = (en, self.cnt[en])
        else:
            tok = (en, self.cnt[en] + self.STEP[en])
        for r in reads:
            if r.rd.get(en, -1) < tok[1]:
                r.rd[en] = tok[1]
        for w in writes:
            w.w = tok
            w.rd = {}
        for w in defer_writer:
            w.w = None
            w.rd = {}
        for w in finalize_writer:
            w.w = tok
            w.rd = {}
        return tok

    def wait(self, en, toks):
        for (e2, c2) in toks:
            if self.wm.get((en, e2), 0) >= c2:
                continue
            self.q[en].append(("w", e2, c2))
            self.wm[(en, e2)] = c2

    def replay(self, en):
        items = self.q[en]
        sems = self.sem
        step = self.STEP[en]
        mysem = sems[en]

        def f(obj):
            for it in items:
                if it[0] == "w":
                    obj.wait_ge(sems[it[1]], it[2])
                else:
                    inst = it[1](obj)
                    if it[2]:
                        inst.then_inc(mysem, step)
        return f


def build_nc():
    nc = bass.Bass("TRN2", target_bir_lowering=False, debug=False)
    dt = nc.dram_tensor
    x_in = dt("x_in", [T, DM], F32, kind="ExternalInput").ap()
    g3 = dt("g3", [NE, 8, 128, 2048], F32, kind="ExternalInput").ap()
    u3 = dt("u3", [NE, 8, 128, 2048], F32, kind="ExternalInput").ap()
    d3 = dt("d3", [NE, 4, 128, 4096], F32, kind="ExternalInput").ap()
    sg3 = dt("sg3", [8, 128, 2048], F32, kind="ExternalInput").ap()
    su3 = dt("su3", [8, 128, 2048], F32, kind="ExternalInput").ap()
    sd3 = dt("sd3", [4, 128, 4096], F32, kind="ExternalInput").ap()
    router_r = dt("router_r", [128, 256], F32, kind="ExternalInput").ap()
    rbias = dt("rbias", [1, NE], F32, kind="ExternalInput").ap()
    qg_in = dt("qg_in", [1, NH], F32, kind="ExternalInput").ap()
    wqT = dt("wqT", [DM, DM], F32, kind="ExternalInput").ap()
    wkT = dt("wkT", [DM, NKV * HD], F32, kind="ExternalInput").ap()
    wvT = dt("wvT", [DM, NKV * HD], F32, kind="ExternalInput").ap()
    wpT = dt("wpT", [DM, DM], F32, kind="ExternalInput").ap()
    cosr = dt("cosr", [128, 128], F32, kind="ExternalInput").ap()
    sinr = dt("sinr", [128, 128], F32, kind="ExternalInput").ap()
    maskc = dt("maskc", [128, 640], F32, kind="ExternalInput").ap()
    identf = dt("identf", [128, 128], F32, kind="ExternalInput").ap()
    onescol = dt("onescol", [1, 128], F32, kind="ExternalInput").ap()
    out = dt("out", [T, DM], F32, kind="ExternalOutput").ap()
    wp_bf = dt("wp_bf", [DM, DM], BF16, kind="Internal").ap()
    dbg = None
    if DEBUG_DUMP == "h":
        dbg = dt("dbg", [128, NDC * T], F32, kind="ExternalOutput").ap()
    elif DEBUG_DUMP == "comb":
        dbg = dt("dbg", [128, NT * 16], F32, kind="ExternalOutput").ap()
    elif DEBUG_DUMP == "a":
        dbg = dt("dbg", [128, 8 * T], F32, kind="ExternalOutput").ap()
    elif DEBUG_DUMP == "ar":
        dbg = dt("dbg", [128, 8 * T], F32, kind="ExternalOutput").ap()
    elif DEBUG_DUMP == "hr":
        dbg = dt("dbg", [128, NDC * T], F32, kind="ExternalOutput").ap()
    elif DEBUG_DUMP == "flash0":
        dbg = dt("dbg", [128, 4128], F32, kind="ExternalOutput").ap()

    with ExitStack() as st:
        e = st.enter_context
        XR = e(nc.sbuf_tensor("XR", [128, NT * DM], BF16))
        HR = e(nc.sbuf_tensor("HR", [128, NDC * T], BF16))
        AR = e(nc.sbuf_tensor("AR", [128, 8 * T], BF16))
        WR = e(nc.sbuf_tensor("WR", [128, 16384], BF16))
        SCR = e(nc.sbuf_tensor("SCR", [128, 2112], F32))
        combR = e(nc.sbuf_tensor("combR", [128, NT * 16], F32))
        statR = e(nc.sbuf_tensor("statR", [128, 160], F32))
        csR = e(nc.sbuf_tensor("csR", [128, 256], BF16))
        mskR = e(nc.sbuf_tensor("mskR", [128, 640], BF16))
        idR = e(nc.sbuf_tensor("idR", [128, 128], BF16))
        qgbR = e(nc.sbuf_tensor("qgbR", [128, 32], F32))
        rowR = e(nc.sbuf_tensor("rowR", [1, 160], F32))
        rtR = e(nc.sbuf_tensor("rtR", [128, 256], BF16))
        ptR = e(nc.sbuf_tensor("ptR", [128, 512], BF16))
        ybR = e(nc.sbuf_tensor("ybR", [128, 256], BF16))
        rpsR = ybR
        diagR = e(nc.sbuf_tensor("diagR", [128, 256], BF16))
        colR = e(nc.sbuf_tensor("colR", [128, 28], F32))

        pb = [e(nc.psum_tensor(f"pb{i}", [128, 512], F32)) for i in range(6)]
        pt = [e(nc.psum_tensor(f"pt{i}", [128, 128], BF16)) for i in range(2)]

        def ptp(i):
            return pt[i % 2][:]

        b = B(nc, st)

        x_rg = [[Rg() for _ in range(4)] for _ in range(NT)]
        h_rg = [[Rg() for _ in range(4)] for _ in range(NDC)]
        a_rg = [[Rg() for _ in range(4)] for _ in range(8)]
        bank_rg = [Rg() for _ in range(8)]
        ptb_rg = [Rg(), Rg()]
        ptb = lambda i: ptb_rg[i % 2]
        wr_pg = [Rg() for _ in range(8)]
        scr_rg = [Rg() for _ in range(4)]
        scrB_rg = [Rg() for _ in range(4)]
        ptr_rg = [Rg() for _ in range(4)]
        m8f_rg = [Rg(), Rg()]
        lp_rg = [Rg(), Rg()]
        diag_rg = [Rg(), Rg()]
        rope_rg = [Rg(), Rg()]
        comb_rg = [Rg() for _ in range(NT)]
        stat_rg = Rg()
        cs_rg, msk_rg, id_rg, qgb_rg, row_rg, rt_rg = (
            Rg(), Rg(), Rg(), Rg(), Rg(), Rg())
        pt_st_rg = [Rg(), Rg()]
        yb_rg = [Rg(), Rg()]
        col_eps = Rg()
        col_sm = [Rg(), Rg()]
        col_rms = [Rg(), Rg()]
        wp_bf_rg = Rg()

        def xv(tt, dc=None):
            base = tt * DM
            if dc is None:
                return XR[:, base:base + DM]
            return XR[:, base + dc * 512: base + (dc + 1) * 512]

        def hv(d, lo, width):
            return HR[:, d * T + lo: d * T + lo + width]

        def wr(pg, lo=0, width=2048):
            return WR[:, pg * 2048 + lo: pg * 2048 + lo + width]

        scores_v = SCR[:, 0:2048]
        ktmp_v = [SCR[:, 0:512], SCR[:, 512:1024]]
        rjunk_v = SCR[:, 1024:1152]
        sg_v = [SCR[:, 0:512], SCR[:, 512:1024]]
        sco_v = lambda tt: SCR[:, 1024 + tt * 16: 1024 + (tt + 1) * 16]
        m8_v = lambda tt: SCR[:, 1536 + tt * 8: 1536 + (tt + 1) * 8]
        rope_v = [SCR[:, 2048:2080], SCR[:, 2080:2112]]
        htm_v = [wr(6), wr(7)]
        scoresB_v = WR[:, 5 * 2048: 7 * 2048].bitcast(F32)
        m8f_v = lambda sm: SCR[:, 2064 + sm * 8: 2064 + sm * 8 + 8]
        lpart_v = lambda sm: SCR[:, 2048 + sm * 4: 2048 + sm * 4 + 4]

        scols = statR[:, 0:16]
        qcols = statR[:, 16:32]
        m16 = statR[:, 32:48]
        msq16 = statR[:, 48:64]
        var16 = statR[:, 64:80]
        std16 = statR[:, 80:96]
        rstd16 = statR[:, 96:112]
        mr16 = statR[:, 112:128]
        nmr16 = statR[:, 128:144]
        ssum16 = statR[:, 144:160]
        epsln_c = colR[:, 0:1]
        epsrms_c = colR[:, 1:2]
        half_c = colR[:, 2:3]
        nm_c = [colR[:, 3:4], colR[:, 4:5]]
        l_c = [colR[:, 5:6], colR[:, 6:7]]
        rl_c = [colR[:, 7:8], colR[:, 8:9]]
        rms4 = [colR[:, 9:13], colR[:, 17:21]]
        rms4b = [colR[:, 13:17], colR[:, 21:25]]

        def dma(dst, src):
            return lambda g, d=dst, s=src: g.dma_start(d, s)

        b.emit("gq", dma(csR[:, 0:128], cosr), writes=[cs_rg])
        b.emit("gq", dma(csR[:, 128:256], sinr), reads=[cs_rg])
        cs_rg.w = ("gq", b.cnt["gq"])
        b.emit("gq", dma(mskR[:], maskc), writes=[msk_rg])
        b.emit("sq", dma(rowR[0:1, 0:128], onescol), writes=[row_rg])
        b.emit("sq", dma(rowR[0:1, 128:128 + NH], qg_in), reads=[row_rg])
        b.emit("sq", dma(rowR[0:1, 128 + NH:128 + 2 * NH], rbias), reads=[row_rg])
        row_rg.w = ("sq", b.cnt["sq"])
        b.emit("gq", dma(idR[:], identf), writes=[id_rg])
        b.emit("gq", dma(rtR[:], router_r), writes=[rt_rg])
        for tt in range(NT):
            b.emit("gq", dma(xv(tt), x_in[tt * 128:(tt + 1) * 128, :]),
                   defer_writer=x_rg[tt], finalize_writer=x_rg[tt])
        b.emit("dve", lambda v: v.memset(epsln_c, LN_EPS), writes=[col_eps])
        b.emit("dve", lambda v: v.memset(epsrms_c, RMS_EPS), reads=[col_eps])
        b.emit("dve", lambda v: v.memset(half_c, 0.5), reads=[col_eps])
        col_eps.w = ("dve", b.cnt["dve"])
        b.emit("pe", lambda t: t.matmul(
            pb[0][:, 0:16], rowR[0:1, 0:128], rowR[0:1, 128:128 + NH],
            start=True, stop=True, skip_group_check=True),
            reads=[row_rg], writes=[bank_rg[0]])
        b.emit("act", lambda s: s.activation(
            qgbR[:, 0:16], pb[0][:, 0:16], AF.Copy, scale=RSQ_HD),
            reads=[bank_rg[0]], writes=[qgb_rg])
        b.emit("pe", lambda t: t.matmul(
            pb[0][:, 0:16], rowR[0:1, 0:128],
            rowR[0:1, 128 + NH:128 + 2 * NH],
            start=True, stop=True, skip_group_check=True),
            reads=[row_rg, qgb_rg], writes=[bank_rg[0]])
        b.emit("dve", lambda v: v.tensor_copy(qgbR[:, 16:32], pb[0][:, 0:16]),
               reads=[bank_rg[0]], finalize_writer=[qgb_rg])
        qgs_col = lambda h: qgbR[:, h:h + 1]
        biasb_v = qgbR[:, 16:32]

        def layer_norm(tts):
            n = len(tts)
            for i, tt in enumerate(tts):
                b.emit("act", lambda s, tt=tt, i=i: s.activation(
                    scores_v, xv(tt), AF.Square, accum_out=qcols[:, i:i + 1]),
                    reads=x_rg[tt],
                    writes=scr_rg if i == 0 else (),
                    defer_writer=[stat_rg] if i == 0 else ())
                b.emit("dve", lambda v, tt=tt, i=i: v.reduce_sum(
                    scols[:, i:i + 1], xv(tt), axis=AX.X),
                    reads=x_rg[tt])
            tok_a = ("act", b.cnt["act"])
            tok_d = ("dve", b.cnt["dve"])
            stat_rg.w = tok_d
            b.emit("dve", lambda v: v.tensor_scalar(
                m16[:, 0:n], scols[:, 0:n], 1.0 / DM, None, op0=ALU.mult),
                reads=[stat_rg], writes=[stat_rg])
            b.wait("act", [tok_d])
            b.emit("act", lambda s: s.activation(msq16[:, 0:n], m16[:, 0:n], AF.Square),
                   reads=[stat_rg], writes=[stat_rg])
            b.wait("dve", [tok_a])
            b.emit("dve", lambda v: v.tensor_scalar(
                var16[:, 0:n], qcols[:, 0:n], 1.0 / DM, None, op0=ALU.mult),
                reads=[stat_rg], writes=[stat_rg])
            b.emit("dve", lambda v: v.tensor_tensor(
                var16[:, 0:n], var16[:, 0:n], msq16[:, 0:n], op=ALU.subtract),
                reads=[stat_rg], writes=[stat_rg])
            b.emit("act", lambda s: s.activation(
                std16[:, 0:n], var16[:, 0:n], AF.Sqrt, bias=epsln_c),
                reads=[stat_rg, col_eps], writes=[stat_rg])
            b.emit("dve", lambda v: v.reciprocal(rstd16[:, 0:n], std16[:, 0:n]),
                   reads=[stat_rg], writes=[stat_rg])
            b.emit("dve", lambda v: v.tensor_tensor(
                mr16[:, 0:n], m16[:, 0:n], rstd16[:, 0:n], op=ALU.mult),
                reads=[stat_rg], writes=[stat_rg])
            b.emit("dve", lambda v: v.tensor_scalar(
                nmr16[:, 0:n], mr16[:, 0:n], -1.0, None, op0=ALU.mult),
                reads=[stat_rg], writes=[stat_rg])
            for i, tt in enumerate(tts):
                hp = i % 2
                b.emit("act", lambda s, tt=tt, i=i, hp=hp: s.activation(
                    htm_v[hp], xv(tt), AF.Identity,
                    bias=nmr16[:, i:i + 1], scale=rstd16[:, i:i + 1]),
                    reads=x_rg[tt] + [stat_rg], writes=[wr_pg[6 + hp]])
                for d in range(NDC):
                    par = d % 2
                    b.emit("pe", lambda t, d=d, par=par, hp=hp: t.transpose(
                        ptp(4 + par), htm_v[hp][:, d * 128:(d + 1) * 128], idR[:]),
                        reads=[wr_pg[6 + hp], id_rg], writes=[ptb(4 + par)])
                    dst = hv(d, i * 128, 128)
                    hreg = h_rg[d][(i * 128) // 512]
                    if d % 2 == 0:
                        b.emit("dve", lambda v, dst=dst, par=par: v.tensor_copy(
                            dst, ptp(4 + par)),
                            reads=[ptb(4 + par)], writes=[hreg])
                    else:
                        b.emit("act", lambda s, dst=dst, par=par: s.copy(
                            dst, ptp(4 + par)),
                            reads=[ptb(4 + par)], writes=[hreg])

        def dump_dbg():
            if DEBUG_DUMP == "h":
                src, regs = HR, [r for row in h_rg for r in row]
            elif DEBUG_DUMP == "comb":
                src, regs = combR, comb_rg
            elif DEBUG_DUMP == "a":
                src, regs = AR, [r for row in a_rg for r in row]
            else:
                return
            b.emit("gq", dma(dbg, src[:]), reads=regs)

        def ffn_phase(phase):
            layer_norm(list(range(NT)))
            if FFN_UPTO == "ln":
                dump_dbg()
                return
            for tt in range(NT):
                for d in range(NDC):
                    b.emit("pe", lambda t, d=d, tt=tt: t.matmul(
                        pb[4][:, 0:16], hv(d, tt * 128, 128),
                        rtR[:, d * 16:(d + 1) * 16],
                        start=(d == 0), stop=(d == NDC - 1),
                        skip_group_check=True),
                        reads=[h_rg[d][(tt * 128) // 512], rt_rg],
                        defer_writer=[bank_rg[4]] if d == 0 else (),
                        finalize_writer=[bank_rg[4]] if d == NDC - 1 else (),
                        inc=(d == NDC - 1))
                b.emit("act", lambda s, tt=tt: s.activation(
                    sco_v(tt), pb[4][:, 0:16], AF.Sigmoid),
                    reads=[bank_rg[4]], writes=[scr_rg[2]] if tt == 0 else (),
                    defer_writer=() if tt == 0 else ())
            tok_sig = ("act", b.cnt["act"])
            scr_rg[2].w = tok_sig
            for tt in range(NT):
                b.emit("dve", lambda v, tt=tt: v.tensor_tensor(
                    combR[:, tt * 16:(tt + 1) * 16], sco_v(tt), biasb_v, op=ALU.add),
                    reads=[scr_rg[2], qgb_rg], writes=[comb_rg[tt]])
            for tt in range(NT):
                b.emit("dve", lambda v, tt=tt: v.max(
                    out=m8_v(tt), in_=combR[:, tt * 16:(tt + 1) * 16]),
                    reads=[comb_rg[tt]], writes=[scr_rg[3]] if tt == 0 else ())
            tok_m8 = ("dve", b.cnt["dve"])
            scr_rg[3].w = tok_m8
            for tt in range(NT):
                b.emit("dve", lambda v, tt=tt: v.tensor_scalar(
                    combR[:, tt * 16:(tt + 1) * 16], combR[:, tt * 16:(tt + 1) * 16],
                    m8_v(tt)[:, 3:4], None, op0=ALU.is_ge),
                    reads=[scr_rg[3], comb_rg[tt]], writes=[comb_rg[tt]])
            for tt in range(NT):
                b.emit("dve", lambda v, tt=tt: v.tensor_tensor(
                    combR[:, tt * 16:(tt + 1) * 16], combR[:, tt * 16:(tt + 1) * 16],
                    sco_v(tt), op=ALU.mult),
                    reads=[scr_rg[2], comb_rg[tt]], writes=[comb_rg[tt]])
            for tt in range(NT):
                b.emit("dve", lambda v, tt=tt: v.reduce_sum(
                    ssum16[:, tt:tt + 1], combR[:, tt * 16:(tt + 1) * 16], axis=AX.X),
                    reads=[comb_rg[tt], stat_rg])
            stat_rg.w = ("dve", b.cnt["dve"])
            b.emit("dve", lambda v: v.reciprocal(ssum16[:], ssum16[:]),
                   reads=[stat_rg], writes=[stat_rg])
            for tt in range(NT):
                b.emit("dve", lambda v, tt=tt: v.tensor_scalar(
                    combR[:, tt * 16:(tt + 1) * 16], combR[:, tt * 16:(tt + 1) * 16],
                    ssum16[:, tt:tt + 1], 0.5, op0=ALU.mult, op1=ALU.mult),
                    reads=[stat_rg, comb_rg[tt]], writes=[comb_rg[tt]])
            if FFN_UPTO == "router":
                dump_dbg()
                return

            items = []
            idx = {}
            for ee in range(NE + 1):
                for fc in range(8):
                    src_g = g3[ee, fc] if ee < NE else sg3[fc]
                    src_u = u3[ee, fc] if ee < NE else su3[fc]
                    k = ee * 8 + fc
                    idx[("g", ee, fc)] = len(items)
                    items.append((wr(k % 2), src_g, [wr_pg[k % 2]]))
                    idx[("u", ee, fc)] = len(items)
                    items.append((wr(2 + k % 2), src_u, [wr_pg[2 + k % 2]]))
                for dc in range(4):
                    src_d = d3[ee, dc] if ee < NE else sd3[dc]
                    k = ee * 4 + dc
                    pga, pgb = 4 + 2 * (k % 2), 5 + 2 * (k % 2)
                    idx[("d", ee, dc)] = len(items)
                    items.append((WR[:, pga * 2048:(pgb + 1) * 2048], src_d,
                                  [wr_pg[pga], wr_pg[pgb]]))
                if phase == 0 and ee == 0:
                    idx["wp"] = len(items)
                    items.append((wp_bf, wpT, [wp_bf_rg]))
            emitted = [0]

            def ensure(i, ahead=3):
                while emitted[0] <= min(i + ahead, len(items) - 1):
                    dst, src, regs = items[emitted[0]]
                    b.emit("gq", dma(dst, src), defer_writer=regs,
                           finalize_writer=regs)
                    emitted[0] += 1

            sel_experts = list(FFN_EXPERTS) if FFN_EXPERTS is not None \
                else list(range(NE + 1))
            for ee in range(NE + 1):
                if ee not in sel_experts:
                    continue
                if ee < NE:
                    c_col = lambda tt, ee=ee: combR[:, tt * 16 + ee:tt * 16 + ee + 1]
                    c_reg = lambda tt: comb_rg[tt]
                else:
                    c_col = lambda tt: half_c
                    c_reg = lambda tt: col_eps
                for fc in range(8):
                    kg = ee * 8 + fc
                    ensure(idx[("g", ee, fc)], ahead=2)
                    ensure(idx[("u", ee, fc)], ahead=2)
                    gp, up_ = wr_pg[kg % 2], wr_pg[2 + kg % 2]
                    gsv, usv = wr(kg % 2), wr(2 + kg % 2)
                    for tc in range(4):
                        pg_ = (fc * 4 + tc) % 2
                        pu_ = 2 + (fc * 4 + tc) % 2
                        for c in range(NDC):
                            b.emit("pe", lambda t, c=c, tc=tc, gsv=gsv, pg_=pg_:
                                   t.matmul(pb[pg_][:],
                                            gsv[:, c * 128:(c + 1) * 128],
                                            hv(c, tc * 512, 512),
                                            start=(c == 0), stop=(c == NDC - 1),
                                            skip_group_check=True),
                                   reads=[gp, h_rg[c][tc]],
                                   defer_writer=[bank_rg[pg_]] if c == 0 else (),
                                   finalize_writer=[bank_rg[pg_]] if c == NDC - 1 else (),
                                   inc=(c == NDC - 1))
                        for c in range(NDC):
                            b.emit("pe", lambda t, c=c, tc=tc, usv=usv, pu_=pu_:
                                   t.matmul(pb[pu_][:],
                                            usv[:, c * 128:(c + 1) * 128],
                                            hv(c, tc * 512, 512),
                                            start=(c == 0), stop=(c == NDC - 1),
                                            skip_group_check=True),
                                   reads=[up_, h_rg[c][tc]],
                                   defer_writer=[bank_rg[pu_]] if c == 0 else (),
                                   finalize_writer=[bank_rg[pu_]] if c == NDC - 1 else (),
                                   inc=(c == NDC - 1))
                        sp = (fc * 4 + tc) % 2
                        b.emit("act", lambda s, pg_=pg_, sp=sp: s.activation(
                            sg_v[sp], pb[pg_][:], AF.Silu),
                            reads=[bank_rg[pg_]], writes=[scr_rg[sp]])
                        a_dst = AR[:, fc * T + tc * 512: fc * T + (tc + 1) * 512]
                        b.emit("dve", lambda v, a_dst=a_dst, sp=sp, pu_=pu_:
                               v.tensor_tensor(a_dst, sg_v[sp], pb[pu_][:],
                                               op=ALU.mult),
                               reads=[scr_rg[sp], bank_rg[pu_]],
                               writes=[a_rg[fc][tc]])
                for dc in range(4):
                    kd = ee * 4 + dc
                    ensure(idx[("d", ee, dc)], ahead=1)
                    pga, pgb = 4 + 2 * (kd % 2), 5 + 2 * (kd % 2)
                    dsv = WR[:, pga * 2048:(pgb + 1) * 2048]
                    for tt in range(NT):
                        pk = (dc * NT + tt) % 2
                        bkv = pb[4 + pk]
                        for fc in range(8):
                            b.emit("pe", lambda t, fc=fc, tt=tt, dsv=dsv, bkv=bkv:
                                   t.matmul(bkv[:],
                                            AR[:, fc * T + tt * 128:
                                               fc * T + (tt + 1) * 128],
                                            dsv[:, fc * 512:(fc + 1) * 512],
                                            start=(fc == 0), stop=(fc == 7),
                                            skip_group_check=True),
                                   reads=[wr_pg[pga], wr_pg[pgb], a_rg[fc][tt // 4]],
                                   defer_writer=[bank_rg[4 + pk]] if fc == 0 else (),
                                   finalize_writer=[bank_rg[4 + pk]] if fc == 7 else (),
                                   inc=(fc == 7))
                        b.emit("dve", lambda v, tt=tt, dc=dc, bkv=bkv, cc=c_col(tt):
                               v.scalar_tensor_tensor(
                                   xv(tt, dc), bkv[:], cc, xv(tt, dc),
                                   op0=ALU.mult, op1=ALU.add),
                               reads=[bank_rg[4 + pk], c_reg(tt), x_rg[tt][dc]],
                               writes=[x_rg[tt][dc]])
            if phase == 0 and FFN_EXPERTS is None:
                ensure(idx["wp"], ahead=0)
            if DEBUG_DUMP == "a":
                dump_dbg()

        def attn_phase():
            if "ffn0" not in RUN_PHASES:
                b.emit("gq", dma(wp_bf, wpT), defer_writer=[wp_bf_rg],
                       finalize_writer=[wp_bf_rg])
            NTH = NT // 2
            v_rg = [Rg() for _ in range(NT)]
            k_rg = [Rg() for _ in range(NKV)]
            q_rg = [Rg() for _ in range(NH)]
            kall = lambda kv_: AR[:, kv_ * T:(kv_ + 1) * T]
            vall = lambda sc: AR[:, NKV * T + sc * 512: NKV * T + (sc + 1) * 512]
            qfm = lambda h_: HR[:, h_ * 2048 + 1024: h_ * 2048 + 2048]
            pbuf = [wr(3), wr(4)]
            pbuf_rg = [wr_pg[3], wr_pg[4]]
            wpsl = [wr(0), wr(1), wr(2)]
            wpsl_rg = [wr_pg[0], wr_pg[1], wr_pg[2]]

            def rms_rope(par, nheads, tt, dst_fn, dst_rg_fn, scale_fn=None):
                src_v = ktmp_v[par]
                src_rg_ = scr_rg[par]
                rms_ = rms4[par]
                rmsb_ = rms4b[par]
                crms = col_rms[par]
                rpv = rope_v[par]
                for h in range(nheads):
                    b.emit("act", lambda s, h=h, src_v=src_v, rms_=rms_:
                           s.activation(
                               rjunk_v, src_v[:, h * 128:(h + 1) * 128],
                               AF.Square, accum_out=rms_[:, h:h + 1]),
                           reads=[src_rg_],
                           writes=[scr_rg[2]] if h == 0 else (),
                           defer_writer=[crms] if h == 0 else ())
                tok = ("act", b.cnt["act"])
                crms.w = tok
                b.emit("dve", lambda v, n=nheads, rms_=rms_: v.tensor_scalar(
                    rms_[:, 0:n], rms_[:, 0:n], 1.0 / HD, epsrms_c,
                    op0=ALU.mult, op1=ALU.add),
                    reads=[crms, col_eps], writes=[crms])
                b.emit("act", lambda s, n=nheads, rms_=rms_, rmsb_=rmsb_:
                       s.activation(rmsb_[:, 0:n], rms_[:, 0:n], AF.Sqrt),
                       reads=[crms], writes=[crms])
                b.emit("dve", lambda v, n=nheads, rms_=rms_, rmsb_=rmsb_:
                       v.reciprocal(rms_[:, 0:n], rmsb_[:, 0:n]),
                       reads=[crms], writes=[crms])
                for h in range(nheads):
                    sp = h % 2
                    stg = rpsR[:, sp * 128:(sp + 1) * 128]
                    b.emit("act", lambda s, h=h, stg=stg, src_v=src_v, rms_=rms_:
                           s.activation(
                               stg, src_v[:, h * 128:(h + 1) * 128], AF.Identity,
                               scale=rms_[:, h:h + 1]),
                           reads=[src_rg_, crms], writes=[pt_st_rg[sp]])
                    cos_v = csR[:, (tt % NT) * 8:(tt % NT) * 8 + 8]
                    sin_v = csR[:, 128 + (tt % NT) * 8:128 + (tt % NT) * 8 + 8]
                    t1, t2 = rpv[:, 0:8], rpv[:, 8:16]
                    t3, t4 = rpv[:, 16:24], rpv[:, 24:32]
                    rrg = rope_rg[par]
                    b.emit("dve", lambda v, stg=stg, cv=cos_v, t1=t1: v.tensor_tensor(
                        t1, stg[:, 0:8], cv, op=ALU.mult),
                        reads=[pt_st_rg[sp], cs_rg], writes=[rrg])
                    b.emit("dve", lambda v, stg=stg, sv=sin_v, t2=t2: v.tensor_tensor(
                        t2, stg[:, 8:16], sv, op=ALU.mult),
                        reads=[pt_st_rg[sp], cs_rg, rrg])
                    b.emit("dve", lambda v, stg=stg, sv=sin_v, t3=t3: v.tensor_tensor(
                        t3, stg[:, 0:8], sv, op=ALU.mult),
                        reads=[pt_st_rg[sp], cs_rg, rrg])
                    b.emit("dve", lambda v, stg=stg, cv=cos_v, t4=t4: v.tensor_tensor(
                        t4, stg[:, 8:16], cv, op=ALU.mult),
                        reads=[pt_st_rg[sp], cs_rg], writes=[rrg])
                    b.emit("dve", lambda v, stg=stg, t1=t1, t2=t2: v.tensor_tensor(
                        stg[:, 0:8], t1, t2, op=ALU.add),
                        reads=[rrg], writes=[pt_st_rg[sp]])
                    b.emit("dve", lambda v, stg=stg, t3=t3, t4=t4: v.tensor_tensor(
                        stg[:, 8:16], t3, t4, op=ALU.subtract),
                        reads=[rrg], writes=[pt_st_rg[sp]])
                    b.emit("pe", lambda t, stg=stg, sp=sp: t.transpose(
                        ptp(4 + sp), stg, idR[:]),
                        reads=[pt_st_rg[sp], id_rg], writes=[ptb(4 + sp)])
                    dst = dst_fn(h)
                    dreg = dst_rg_fn(h)
                    if scale_fn is None:
                        b.emit("dve", lambda v, dst=dst, sp=sp: v.tensor_copy(
                            dst, ptp(4 + sp)),
                            reads=[ptb(4 + sp)], writes=[dreg])
                    else:
                        sc_col = scale_fn(h)
                        b.emit("dve", lambda v, dst=dst, sp=sp, sc_col=sc_col:
                               v.tensor_scalar(dst, ptp(4 + sp), sc_col, None,
                                               op0=ALU.mult),
                               reads=[ptb(4 + sp), qgb_rg], writes=[dreg])

            def dump_attn(v_rg, k_rg, q_rg):
                if DEBUG_DUMP == "ar":
                    b.emit("gq", dma(dbg, AR[:]), reads=v_rg + k_rg)
                elif DEBUG_DUMP == "hr":
                    b.emit("gq", dma(dbg, HR[:]), reads=q_rg)

            for half in ATTN_HALVES:
                tts = list(range(half * NTH, (half + 1) * NTH))
                layer_norm(tts)
                for d in range(NDC):
                    pgi = d // 4
                    b.emit("gq", dma(wr(pgi, (d % 4) * 512, 512),
                                     wkT[d * 128:(d + 1) * 128, :]),
                           defer_writer=[wr_pg[pgi]] if d % 4 == 0 else (),
                           finalize_writer=[wr_pg[pgi]] if d % 4 == 3 else ())
                for d in range(NDC):
                    pgi = 4 + d // 4
                    b.emit("gq", dma(wr(pgi, (d % 4) * 512, 512),
                                     wvT[d * 128:(d + 1) * 128, :]),
                           defer_writer=[wr_pg[pgi]] if d % 4 == 0 else (),
                           finalize_writer=[wr_pg[pgi]] if d % 4 == 3 else ())

                def proj(i, tt, wtile_fn, wreg_fn, pk0, evac):
                    pk = pk0 + i % 2
                    for d in range(NDC):
                        b.emit("pe", lambda t, d=d, i=i, pk=pk: t.matmul(
                            pb[pk][:], hv(d, i * 128, 128), wtile_fn(d),
                            start=(d == 0), stop=(d == NDC - 1),
                            skip_group_check=True),
                            reads=[wreg_fn(d), h_rg[d][(i * 128) // 512]],
                            defer_writer=[bank_rg[pk]] if d == 0 else (),
                            finalize_writer=[bank_rg[pk]] if d == NDC - 1 else (),
                            inc=(d == NDC - 1))
                    evac(pk)

                for i, tt in enumerate(tts):
                    def evac_k(pk, i=i, tt=tt):
                        par = i % 2
                        b.emit("act", lambda s, pk=pk, par=par: s.copy(
                            ktmp_v[par], pb[pk][:]),
                            reads=[bank_rg[pk]], writes=[scr_rg[par]])
                        rms_rope(par, NKV, tt,
                                 lambda kv_, half=half, i=i: kall(kv_)[
                                     :, half * 1024 + i * 128:
                                     half * 1024 + i * 128 + 128],
                                 lambda kv_: k_rg[kv_])
                    proj(i, tt, lambda d: wr(d // 4, (d % 4) * 512, 512),
                         lambda d: wr_pg[d // 4], 0, evac_k)
                for i, tt in enumerate(tts):
                    def evac_v(pk, tt=tt):
                        b.emit("act", lambda s, pk=pk, tt=tt: s.copy(
                            vall(tt), pb[pk][:]),
                            reads=[bank_rg[pk]], writes=[v_rg[tt]])
                    proj(i, tt, lambda d: wr(4 + d // 4, (d % 4) * 512, 512),
                         lambda d: wr_pg[4 + d // 4], 0, evac_v)
                for fq in range(4):
                    par = fq % 2
                    for d in range(NDC):
                        pgi = par * 4 + d // 4
                        b.emit("gq", dma(wr(pgi, (d % 4) * 512, 512),
                                         wqT[d * 128:(d + 1) * 128,
                                             fq * 512:(fq + 1) * 512]),
                               defer_writer=[wr_pg[pgi]] if d % 4 == 0 else (),
                               finalize_writer=[wr_pg[pgi]] if d % 4 == 3 else ())
                    for i, tt in enumerate(tts):
                        def evac_q(pk, i=i, tt=tt, fq=fq):
                            par = i % 2
                            b.emit("act", lambda s, pk=pk, par=par: s.copy(
                                ktmp_v[par], pb[pk][:]),
                                reads=[bank_rg[pk]], writes=[scr_rg[par]])
                            rms_rope(par, 4, tt,
                                     lambda j, fq=fq, i=i: qfm(fq * 4 + j)[
                                         :, i * 128:(i + 1) * 128],
                                     lambda j, fq=fq: q_rg[fq * 4 + j],
                                     scale_fn=lambda j, fq=fq:
                                     qgs_col(fq * 4 + j))
                        proj(i, tt, lambda d, par=par: wr(
                            par * 4 + d // 4, (d % 4) * 512, 512),
                            lambda d, par=par: wr_pg[par * 4 + d // 4], 2, evac_q)

                if ATTN_UPTO == "proj":
                    dump_attn(v_rg, k_rg, q_rg)
                    continue

                def load_wp(head, slot):
                    b.emit("sq", dma(wpsl[slot],
                                     wp_bf[head * 128:(head + 1) * 128, :]),
                           reads=[wp_bf_rg], defer_writer=[wpsl_rg[slot]],
                           finalize_writer=[wpsl_rg[slot]])

                diagv = lambda pp: diagR[:, pp * 128:(pp + 1) * 128]

                def sc_chunk(sm, c, W):
                    if sm == 0:
                        return (scores_v[:, c * 512:c * 512 + W], scr_rg[c], [])
                    pg = 5 if c < 2 else 6
                    return (scoresB_v[:, c * 512:c * 512 + W], scrB_rg[c],
                            [wr_pg[pg]])

                ec_ctr = [0]
                pt_ctr = [0]

                for qt in range(NTH):
                    gqt = half * NTH + qt
                    Kend = (gqt + 1) * 128
                    nkc = (Kend + 511) // 512
                    nsc = Kend // 128

                    def stage_A1(head, Kend=Kend, nkc=nkc, qt=qt):
                        if head >= NH:
                            return []
                        ops = []
                        kv_ = head // 4
                        sm = head % 2
                        pp = head % 2
                        for c in range(nkc):
                            W = min(512, Kend - c * 512)
                            ops.append(dict(
                                en="pe",
                                fn=lambda t, head=head, c=c, W=W, kv_=kv_,
                                qt=qt:
                                t.matmul(pb[0][:, 0:W],
                                         qfm(head)[:, qt * 128:(qt + 1) * 128],
                                         kall(kv_)[:, c * 512:c * 512 + W],
                                         start=True, stop=True,
                                         skip_group_check=True),
                                reads=[q_rg[head], k_rg[kv_]],
                                writes=[bank_rg[0]]))
                            dstv, dreg, pgex = sc_chunk(sm, c, W)
                            if c == nkc - 1:
                                msl = mskR[:, 640 - W:640]
                                ops.append(dict(
                                    en="dve",
                                    fn=lambda v, dstv=dstv, W=W, msl=msl:
                                    v.tensor_tensor(dstv, pb[0][:, 0:W],
                                                    msl, op=ALU.add),
                                    reads=[bank_rg[0], msk_rg],
                                    writes=[dreg] + pgex))
                            else:
                                ev = "dve" if ec_ctr[0] % 2 == 0 else "act"
                                ec_ctr[0] += 1
                                if ev == "dve":
                                    ops.append(dict(
                                        en="dve",
                                        fn=lambda v, dstv=dstv, W=W:
                                        v.tensor_copy(dstv, pb[0][:, 0:W]),
                                        reads=[bank_rg[0]],
                                        writes=[dreg] + pgex))
                                else:
                                    ops.append(dict(
                                        en="act",
                                        fn=lambda s, dstv=dstv, W=W:
                                        s.copy(dstv, pb[0][:, 0:W]),
                                        reads=[bank_rg[0]],
                                        writes=[dreg] + pgex))
                            ops.append(dict(
                                en="dve",
                                fn=lambda v, dstv=dstv, sm=sm, c=c:
                                v.reduce_max(m8f_v(sm)[:, c:c + 1], dstv,
                                             axis=AX.X),
                                reads=[dreg], writes=[m8f_rg[sm]]))
                        return ops

                    def stage_A2(head, Kend=Kend, nkc=nkc, qt=qt):
                        if head >= NH:
                            return []
                        ops = []
                        sm = head % 2
                        pp = head % 2
                        ops.append(dict(
                            en="dve",
                            fn=lambda v, sm=sm, nkc=nkc: v.reduce_max(
                                nm_c[sm], m8f_v(sm)[:, 0:nkc], axis=AX.X,
                                negate=True),
                            reads=[m8f_rg[sm]], writes=[col_sm[sm]]))
                        for c in range(nkc):
                            W = min(512, Kend - c * 512)
                            srcv, sreg, pgex = sc_chunk(sm, c, W)
                            ops.append(dict(
                                en="act",
                                fn=lambda s, srcv=srcv, c=c, W=W, pp=pp, sm=sm:
                                s.activation(pbuf[pp][:, c * 512:c * 512 + W],
                                             srcv, AF.Exp, bias=nm_c[sm],
                                             accum_out=lpart_v(sm)[:, c:c + 1]),
                                reads=[col_sm[sm], sreg] + pgex,
                                writes=[lp_rg[sm]],
                                defer_writer=[pbuf_rg[pp]] if c == 0 else (),
                                finalize_writer=[pbuf_rg[pp]]
                                if c == nkc - 1 else ()))
                        ops.append(dict(
                            en="dve",
                            fn=lambda v, sm=sm, nkc=nkc: v.reduce_sum(
                                l_c[sm], lpart_v(sm)[:, 0:nkc], axis=AX.X),
                            reads=[lp_rg[sm]], writes=[col_sm[sm]]))
                        ops.append(dict(
                            en="dve",
                            fn=lambda v, sm=sm: v.reciprocal(rl_c[sm], l_c[sm]),
                            reads=[col_sm[sm]], writes=[col_sm[sm]]))
                        return ops

                    def stage_B(head, nsc=nsc):
                        ops = []
                        kv_ = head // 4
                        pp = head % 2
                        sm = head % 2
                        for sc in range(nsc):
                            slot = pt_ctr[0] % 4
                            pt_ctr[0] += 1
                            ops.append(dict(
                                en="pe",
                                fn=lambda t, sc=sc, slot=slot, pp=pp: t.transpose(
                                    ptp(slot),
                                    pbuf[pp][:, sc * 128:(sc + 1) * 128],
                                    idR[:]),
                                reads=[pbuf_rg[pp], id_rg],
                                writes=[ptb(slot)]))
                            if slot % 2 == 0:
                                ops.append(dict(
                                    en="dve",
                                    fn=lambda v, slot=slot: v.tensor_copy(
                                        ptR[:, slot * 128:(slot + 1) * 128],
                                        ptp(slot)),
                                    reads=[ptb(slot)], writes=[ptr_rg[slot]]))
                            else:
                                ops.append(dict(
                                    en="act",
                                    fn=lambda s, slot=slot: s.copy(
                                        ptR[:, slot * 128:(slot + 1) * 128],
                                        ptp(slot)),
                                    reads=[ptb(slot)], writes=[ptr_rg[slot]]))
                            ops.append(dict(
                                en="pe",
                                fn=lambda t, sc=sc, slot=slot, kv_=kv_, nsc=nsc:
                                t.matmul(pb[1][:, 0:128],
                                         ptR[:, slot * 128:(slot + 1) * 128],
                                         vall(sc)[:, kv_ * 128:(kv_ + 1) * 128],
                                         start=(sc == 0), stop=(sc == nsc - 1),
                                         skip_group_check=True),
                                reads=[v_rg[sc], ptr_rg[slot]],
                                defer_writer=[bank_rg[1]] if sc == 0 else (),
                                finalize_writer=[bank_rg[1]] if sc == nsc - 1
                                else ()))
                        ops.append(dict(
                            en="act",
                            fn=lambda s, pp=pp, sm=sm: s.activation(
                                diagv(pp), pb[1][:, 0:128], AF.Copy,
                                scale=rl_c[sm]),
                            reads=[bank_rg[1], col_sm[sm]],
                            writes=[diag_rg[pp]]))
                        slot2 = pt_ctr[0] % 4
                        pt_ctr[0] += 1
                        ops.append(dict(
                            en="pe",
                            fn=lambda t, pp=pp, slot2=slot2: t.transpose(
                                ptp(slot2), diagv(pp), idR[:]),
                            reads=[diag_rg[pp], id_rg], writes=[ptb(slot2)]))
                        ops.append(dict(
                            en="dve",
                            fn=lambda v, pp=pp, slot2=slot2: v.tensor_copy(
                                ybR[:, pp * 128:(pp + 1) * 128], ptp(slot2)),
                            reads=[ptb(slot2)], writes=[yb_rg[pp]]))
                        slot_h = head % 3
                        for f in range(4):
                            ops.append(dict(
                                en="pe",
                                fn=lambda t, f=f, head=head, slot_h=slot_h,
                                pp=pp:
                                t.matmul(pb[2 + f][:],
                                         ybR[:, pp * 128:(pp + 1) * 128],
                                         wpsl[slot_h][:, f * 512:(f + 1) * 512],
                                         start=(head == 0), stop=(head == NH - 1),
                                         skip_group_check=True),
                                reads=[yb_rg[pp], wpsl_rg[slot_h]],
                                defer_writer=[bank_rg[2 + f]] if head == 0
                                else (),
                                finalize_writer=[bank_rg[2 + f]]
                                if head == NH - 1 else ()))
                        return ops

                    def emit_op(op):
                        b.emit(op["en"], op["fn"], reads=op.get("reads", ()),
                               writes=op.get("writes", ()),
                               defer_writer=op.get("defer_writer", ()),
                               finalize_writer=op.get("finalize_writer", ()))

                    if qt == 0:
                        load_wp(0, 0)
                        load_wp(1, 1)
                    for op in stage_A1(0) + stage_A2(0) + stage_A1(1):
                        emit_op(op)
                    for head in range(NH):
                        nxt = stage_A1(head + 2)
                        bops = stage_B(head)
                        ai = 0
                        for j, op in enumerate(bops):
                            emit_op(op)
                            if j % 2 == 1 and ai < len(nxt):
                                emit_op(nxt[ai])
                                ai += 1
                        for op in nxt[ai:]:
                            emit_op(op)
                        for op in stage_A2(head + 1):
                            emit_op(op)
                        if head + 2 < NH:
                            load_wp(head + 2, (head + 2) % 3)
                    if qt + 1 < NTH:
                        load_wp(0, 0)
                        load_wp(1, 1)
                    for f in range(4):
                        b.emit("dve", lambda v, f=f, gqt=gqt: v.tensor_tensor(
                            xv(gqt, f), xv(gqt, f), pb[2 + f][:], op=ALU.add),
                            reads=[bank_rg[2 + f], x_rg[gqt][f]],
                            writes=[x_rg[gqt][f]])

        for ph in RUN_PHASES:
            if ph == "ffn0":
                ffn_phase(0)
            elif ph == "attn":
                attn_phase()
            elif ph == "ffn1":
                ffn_phase(1)

        for tt in range(NT):
            b.emit("gq", dma(out[tt * 128:(tt + 1) * 128, :], xv(tt)),
                   reads=x_rg[tt])
        b.wait("gq", [("gq", b.cnt["gq"])])

        blk = e(nc.Block())
        blk.tensor(b.replay("pe"))
        blk.scalar(b.replay("act"))
        blk.vector(b.replay("dve"))
        blk.gpsimd(b.replay("gq"))
        blk.sync(b.replay("sq"))
        build_nc.last_b = b
    return nc


def _prep_inputs(inp):
    asc = np.ascontiguousarray
    f32 = lambda a: np.asarray(a, np.float32)

    def relay_gu3(wT3):
        E_ = wT3.shape[0]
        return asc(wT3.reshape(E_, 16, 128, 8, 128).transpose(0, 3, 2, 1, 4)
                   .reshape(E_, 8, 128, 2048))

    def relay_dn3(wT3):
        E_ = wT3.shape[0]
        return asc(wT3.reshape(E_, 8, 128, 4, 512).transpose(0, 3, 2, 1, 4)
                   .reshape(E_, 4, 128, 4096))

    gateT = f32(inp["L0_ffn_expert_gate_w"]).transpose(0, 2, 1)
    upT = f32(inp["L0_ffn_expert_up_w"]).transpose(0, 2, 1)
    downT = f32(inp["L0_ffn_expert_down_w"]).transpose(0, 2, 1)
    g3 = relay_gu3(gateT)
    u3 = relay_gu3(upT)
    d3 = relay_dn3(downT)
    sg3 = relay_gu3(f32(inp["L0_ffn_shared_gate_w"]).T[None])[0]
    su3 = relay_gu3(f32(inp["L0_ffn_shared_up_w"]).T[None])[0]
    sd3 = relay_dn3(f32(inp["L0_ffn_shared_down_w"]).T[None])[0]
    router_r = asc(f32(inp["L0_ffn_router_w"]).T.reshape(16, 128, 16)
                   .transpose(1, 0, 2).reshape(128, 256))

    t = np.arange(T, dtype=np.float32)
    inv = 1.0 / (10000.0 ** (np.arange(0, PD, 2, dtype=np.float32) / PD))
    f = t[:, None] * inv[None, :]
    cosr = asc(np.cos(f).astype(np.float32).reshape(16, 128, 8)
               .transpose(1, 0, 2).reshape(128, 128))
    sinr = asc(np.sin(f).astype(np.float32).reshape(16, 128, 8)
               .transpose(1, 0, 2).reshape(128, 128))
    rr, cc = np.meshgrid(np.arange(128), np.arange(128), indexing="ij")
    maskc = np.concatenate(
        [np.zeros((128, 512), np.float32),
         np.where(cc <= rr, 0.0, -1e30).astype(np.float32)], axis=1)
    common = dict(
        g3=g3, u3=u3, d3=d3, sg3=sg3, su3=su3, sd3=sd3, router_r=router_r,
        rbias=f32(inp["L0_ffn_router_bias"]).reshape(1, NE),
        qg_in=f32(inp["L0_attn_qg"]).reshape(1, NH),
        wqT=asc(f32(inp["L0_attn_wq"]).T),
        wkT=asc(f32(inp["L0_attn_wk"]).T),
        wvT=asc(f32(inp["L0_attn_wv"]).T),
        wpT=asc(f32(inp["L0_attn_wp"]).T),
        cosr=cosr, sinr=sinr, maskc=maskc,
        identf=np.eye(128, dtype=np.float32),
        onescol=np.ones((1, 128), np.float32),
    )
    x = f32(inp["L0_attn_x"])
    in_maps = []
    for c in range(8):
        m = dict(common)
        m["x_in"] = asc(x[c])
        in_maps.append(m)
    return in_maps


def kernel(**inputs):
    in_maps = _prep_inputs(inputs)
    nc = build_nc()
    res = run_bass_kernel_spmd(nc, in_maps, core_ids=list(range(8)))
    outs = [np.asarray(res.results[c]["out"]) for c in range(8)]
    return np.stack(outs, axis=0).astype(np.float32)
